# version 15
# baseline (speedup 1.0000x reference)
import math
from contextlib import ExitStack
import numpy as np
import ml_dtypes
import concourse.bass as bass
import concourse.mybir as mybir
import concourse.bass_utils as _bu
from concourse.bass_utils import run_bass_kernel_spmd

if not getattr(_bu, "_ldwopt_patched", False):
    _orig_run_command = _bu.run_command

    def _run_command_ldw(argv, **kw):
        argv = ["--enable-ldw-opt=true" if a == "--enable-ldw-opt=false" else a
                for a in argv]
        if any(a.startswith("--enable-ldw-opt") for a in argv):
            argv = argv[:-2] + ["--enable-separate-load-and-compute=true"] + argv[-2:]
        return _orig_run_command(argv, **kw)

    _bu.run_command = _run_command_ldw
    _bu._ldwopt_patched = True

F32 = mybir.dt.float32
BF16 = mybir.dt.bfloat16
I32 = mybir.dt.int32
AF = mybir.ActivationFunctionType
ALU = mybir.AluOpType

B, N, D, H, DH, HID = 8, 2048, 2048, 8, 256, 5632
NT = N // 128
DC = D // 128
HC = HID // 128
EPS = 1e-5

_TAB = {}


def _host_tables():
    if _TAB:
        return _TAB
    f8 = np.float64
    xs = np.linspace(0.0, 1.0, DH // 2)
    thetas = np.repeat(1.0 / (10000.0 ** xs), 2)
    pos = np.arange(N, dtype=f8)
    ang = pos[:, None] * thetas[None, :]
    cosT = np.cos(ang).T
    sinT = np.sin(ang).T
    lgam = np.log(1.0 - np.exp(np.linspace(math.log(1 / 32), math.log(1 / 512), H)))
    gq = np.exp(pos[None, :] * lgam[:, None])
    gk = np.exp(-pos[None, :] * lgam[:, None]) / 16.0
    cosq = np.empty((D, N), np.float32); sinq = np.empty((D, N), np.float32)
    cosk = np.empty((D, N), np.float32); sink = np.empty((D, N), np.float32)
    sgn = np.where(np.arange(DH) % 2 == 0, -1.0, 1.0)[:, None]
    for h in range(H):
        sl = slice(h * DH, (h + 1) * DH)
        cosq[sl] = (cosT * gq[h][None, :]).astype(np.float32)
        sinq[sl] = (sgn * sinT * gq[h][None, :]).astype(np.float32)
        cosk[sl] = (cosT * gk[h][None, :]).astype(np.float32)
        sink[sl] = (sgn * sinT * gk[h][None, :]).astype(np.float32)
    tri = (np.arange(128)[None, :] >= np.arange(128)[:, None]).astype(np.float32)
    perm = np.zeros((128, 128), np.float32)
    for i in range(64):
        perm[2 * i, 2 * i + 1] = 1.0
        perm[2 * i + 1, 2 * i] = -1.0
    bfc = ml_dtypes.bfloat16
    _TAB.update(dict(cosq=cosq.astype(bfc), sinq=sinq.astype(bfc),
                     cosk=cosk.astype(bfc), sink=sink.astype(bfc), trimask=tri,
                     perm=perm.astype(ml_dtypes.bfloat16),
                     ones1=np.ones((1, 128), ml_dtypes.bfloat16),
                     onesf=np.ones((1, 1), np.float32)))
    return _TAB


class Bld:
    def __init__(self, nc):
        self.nc = nc
        self.sem = {}
        self.cnt = {}
        self.hw = {}
        self._stack = []

    def mksem(self, name):
        cm = self.nc.semaphore(name)
        self.sem[name] = cm.__enter__()
        self._stack.append(cm)
        self.cnt[name] = 0

    def close(self):
        for cm in reversed(self._stack):
            cm.__exit__(None, None, None)

    def inc(self, instr, s, step=1):
        self.cnt[s] += step
        instr.then_inc(self.sem[s], step)
        return self.cnt[s]

    def dma(self, eng, s, out, in_):
        eng.dma_start(out=out, in_=in_).then_inc(self.sem[s], 16)
        self.cnt[s] += 16
        return self.cnt[s]

    def dmat(self, eng, s, out, in_):
        eng.dma_start_transpose(out=out, in_=in_).then_inc(self.sem[s], 16)
        self.cnt[s] += 16
        return self.cnt[s]

    def w(self, eng, s, val=None):
        if val is None:
            val = self.cnt[s]
        if val <= 0:
            return
        k = (id(eng), s)
        if self.hw.get(k, -1) >= val:
            return
        eng.wait_ge(self.sem[s], val)
        self.hw[k] = val


def build(upto=5, dbg=()):
    nc = bass.Bass("TRN2", target_bir_lowering=False, debug=False)
    t, a, v, p, sy = nc.tensor, nc.scalar, nc.vector, nc.gpsimd, nc.sync

    def dram(name, shape, dt, kind):
        if name in dbg and kind == "Internal":
            kind = "ExternalOutput"
        return nc.dram_tensor(name, shape, dt, kind=kind).ap()

    x_in = dram("x", [N, D], F32, "ExternalInput")
    W = {nm: dram(nm, [D, D], F32, "ExternalInput") for nm in ("wq", "wk", "wv", "wg", "wo")}
    row_in = {nm: dram(nm, [1, D], F32, "ExternalInput")
              for nm in ("bq_r", "bk_r", "bg_r", "bv_r", "bo_r", "gnw_r", "gnb_r", "sd_r")}
    row_in["sg_r"] = dram("sg_r", [1, HID], F32, "ExternalInput")
    row_in["su_r"] = dram("su_r", [1, HID], F32, "ExternalInput")
    wq_int = {"g": dram("wqg", [HID, D], I32, "ExternalInput"),
              "u": dram("wqu", [HID, D], I32, "ExternalInput"),
              "d": dram("wqd", [D, HID], I32, "ExternalInput")}
    tabs = {nm: dram(nm, [D, N], BF16, "ExternalInput") for nm in ("cosq", "sinq", "cosk", "sink")}
    trimask_in = dram("trimask", [128, 128], F32, "ExternalInput")
    perm_in = dram("perm", [128, 128], BF16, "ExternalInput")
    ones_in = dram("ones1", [1, 128], BF16, "ExternalInput")
    onesf_in = dram("onesf", [1, 1], F32, "ExternalInput")

    out = dram("out", [N, D], F32, "ExternalOutput")
    wbf = {"g": dram("wg_bf", [HID, D], BF16, "Internal"),
           "u": dram("wu_bf", [HID, D], BF16, "Internal"),
           "d": dram("wd_bf", [D, HID], BF16, "Internal")}
    x_bf = dram("x_bf", [N, D], BF16, "Internal")
    watt = {nm: dram("a" + nm + "_bf", [D, D], BF16, "Internal") for nm in ("wq", "wk", "wv", "wg", "wo")}
    qf = dram("qf", [DC, 128, N], BF16, "Internal")
    kf = dram("kf", [DC, 128, N], BF16, "Internal")
    gf = dram("gf", [DC, 128, N], BF16, "Internal")
    v_tm = dram("v_tm", [N, D], BF16, "Internal")
    retn = dram("retn", [N, D], BF16, "Internal")
    y32 = dram("y32", [N, D], F32, "Internal")
    ybf = dram("ybf", [N, D], BF16, "Internal")

    pers = ExitStack()

    def sbp(name, shape, dt, **kw):
        return pers.enter_context(nc.sbuf_tensor(name, shape, dt, **kw))

    c_eps = nc.alloc_sbuf_tensor("c_eps", [128, 1], F32)
    c_tri = sbp("c_tri", [128, 128], F32)
    c_perm = sbp("c_perm", [128, 128], BF16)
    c_ones = sbp("c_ones", [1, 128], BF16)
    c_onef = sbp("c_onef", [1, 1], F32)
    col = {nm: sbp("c_" + nm, [128, DC], F32) for nm in ("bq", "bk", "bg", "gnw", "gnb", "sd")}
    col["sg"] = sbp("c_sg", [128, HC], F32)
    col["su"] = sbp("c_su", [128, HC], F32)
    colm = {"sg": sbp("cm_sg", [128, HC], F32), "su": sbp("cm_su", [128, HC], F32),
            "sd": sbp("cm_sd", [128, DC], F32)}
    bv_bc = sbp("c_bv_bc", [128, D], F32)
    bo_bc = sbp("c_bo_bc", [128, D], F32)
    p0 = ExitStack()
    rows = {nm: p0.enter_context(nc.sbuf_tensor(
        "r_" + nm, [1, HID if nm in ("sg_r", "su_r") else D], F32)) for nm in row_in}
    onesf_row = p0.enter_context(nc.sbuf_tensor("c_onesf_row", [1, 128], F32))

    psA = [nc.alloc_psum_tensor("psA%d" % i, [128, 512], F32) for i in range(4)]
    psR = [nc.alloc_psum_tensor("psR0", [128, 512], F32),
           nc.alloc_psum_tensor("psR1", [128, 512], F32)]
    psm_scope = ExitStack()
    psSm = psm_scope.enter_context(nc.psum_tensor("psSm", [128, 64], F32))

    def ret_ps(qs, hb=0):
        if hb == 0:
            return psA[2 + qs // 2][:, (qs % 2) * 256:(qs % 2) * 256 + 256]
        return psRetB[2 + qs // 2][:, (qs % 2) * 256:(qs % 2) * 256 + 256]

    bd = Bld(nc)
    for s in ("pe", "act", "dve", "pool",
              "dcst", "dldA", "dldB", "dld2A", "dld2B", "dld2C", "dld2D", "dgA", "dgB", "dgC", "dgD",
              "dsp", "dspA", "dspB", "dspC", "dspD", "p1a", "p1b", "p1c", "p1d", "kv0", "kv1", "kv2", "kv3", "dtr", "dtrA", "dtrB", "dst"):
        bd.mksem(s)
    SEMS = list(bd.sem.keys())
    ENG = [t, a, v, p, sy]

    def barrier():
        for e in ENG:
            for s in SEMS:
                bd.w(e, s)

    wcast_done = {}
    if upto >= 2:
        for tb in range(4):
            bd.dma(p, "dsp", x_bf[tb * 512:(tb + 1) * 512, :], x_in[tb * 512:(tb + 1) * 512, :])
        for nm in ("wq", "wk", "wg", "wv", "wo"):
            for ch in range(4):
                bd.dma(p, "dgA", watt[nm][ch * 512:(ch + 1) * 512, :],
                       W[nm][ch * 512:(ch + 1) * 512, :])
            wcast_done[nm] = bd.cnt["dgA"]
    for dst, src in [(c_tri[:], trimask_in[:]), (c_perm[:], perm_in[:]),
                     (c_ones[:], ones_in[:]), (c_onef[:], onesf_in[:])] + \
                    [(rows[nm][:], row_in[nm][:]) for nm in rows]:
        bd.dma(sy, "dcst", dst, src)
    bd.w(t, "dcst")
    bd.w(v, "dcst")
    for rnm, cdst, nch in [("bq_r", col["bq"], DC), ("bk_r", col["bk"], DC),
                           ("bg_r", col["bg"], DC), ("gnw_r", col["gnw"], DC),
                           ("gnb_r", col["gnb"], DC), ("sd_r", col["sd"], DC),
                           ("sg_r", col["sg"], HC), ("su_r", col["su"], HC)]:
        pe_t = 0
        for c in range(nch):
            mm = t.matmul(psSm[:, c:c + 1], rows[rnm][0:1, c * 128:(c + 1) * 128],
                          c_onef[0:1, 0:1], start=(c == 0), stop=(c == nch - 1))
            if c == nch - 1:
                pe_t = bd.inc(mm, "pe")
        bd.w(a, "pe", pe_t)
        act_t = bd.inc(a.activation(cdst[:], psSm[:, 0:nch], AF.Identity), "act")
        bd.w(t, "act", act_t)
    bd.w(v, "act")
    for nm in ("sg", "su", "sd"):
        src = col[nm] if nm != "sd" else col["sd"]
        bd.inc(v.tensor_scalar(colm[nm][:], src[:], -128.0, None, ALU.mult), "dve")
    dvo = bd.inc(v.memset(onesf_row[:], 1.0), "dve")
    bd.inc(v.memset(c_eps[:], EPS), "dve")
    bd.w(t, "dve", dvo)
    for rnm, bdst in (("bv_r", bv_bc), ("bo_r", bo_bc)):
        for j in range(4):
            pe_t = bd.inc(t.matmul(psA[0][:], onesf_row[:], rows[rnm][0:1, j * 512:(j + 1) * 512],
                                   start=True, stop=True), "pe")
            bd.w(a, "pe", pe_t)
            act_t = bd.inc(a.activation(bdst[:, j * 512:(j + 1) * 512], psA[0][:], AF.Identity),
                           "act")
            bd.w(t, "act", act_t)
    barrier()
    p0.close()
    psm_scope.close()
    psRetB = [None, None,
              nc.alloc_psum_tensor("psRB2", [128, 512], F32),
              nc.alloc_psum_tensor("psRB3", [128, 512], F32)]

    x_scope = ExitStack()

    def sb(name, shape, dt, **kw):
        return x_scope.enter_context(nc.sbuf_tensor(name, shape, dt, **kw))

    p1_jobs = []
    for m in ("g", "u"):
        for c in range(HC):
            p1_jobs.append((m, c, 0, D))
    for c in range(DC):
        for off, wd_ in ((0, 2048), (2048, 2048), (4096, 1536)):
            p1_jobs.append(("d", c, off, wd_))
    p1_state = dict(ji=0, li=0, war_i=[0, 0, 0, 0], war_o=[0, 0])
    i1 = [sb("p1_i%d" % i, [128, 2048], BF16) for i in range(4)]
    o1 = [sb("p1_o0", [128, 2048], BF16), sb("p1_o1", [128, 2048], BF16)]
    P1LS = ("p1a", "p1b", "p1c", "p1d")
    bd.w(a, "dve")

    def p1_load():
        li = p1_state["li"]
        if li >= len(p1_jobs):
            return
        p1_state["li"] += 1
        m, ch, off, wd_ = p1_jobs[li]
        lp = li % 4
        if p1_state["war_i"][lp]:
            bd.w(p, "act", p1_state["war_i"][lp])
        bd.dma(p, P1LS[lp], i1[lp][:, 0:wd_], wq_int[m][ch * 128:(ch + 1) * 128, off:off + wd_])

    def p1_step():
        while p1_state["li"] < min(p1_state["ji"] + 4, len(p1_jobs)):
            p1_load()
        ji = p1_state["ji"]
        if ji >= len(p1_jobs):
            return
        p1_state["ji"] += 1
        m, ch, off, wd_ = p1_jobs[ji]
        lp = ji % 4
        pp = ji % 2
        bd.w(a, P1LS[lp])
        ss_ = "dspC" if pp == 0 else "dspD"
        if p1_state["war_o"][pp]:
            bd.w(a, ss_)
        sc, scm = (col["sg"], colm["sg"]) if m == "g" else \
                  (col["su"], colm["su"]) if m == "u" else (col["sd"], colm["sd"])
        act_t = bd.inc(a.activation(o1[pp][:, 0:wd_], i1[lp][:, 0:wd_], AF.Identity,
                                    bias=scm[:, ch:ch + 1], scale=sc[:, ch:ch + 1]), "act")
        p1_state["war_i"][lp] = act_t
        bd.w(p, "act", act_t)
        p1_state["war_o"][pp] = bd.dma(p, ss_, wbf[m][ch * 128:(ch + 1) * 128, off:off + wd_],
                                       o1[pp][:, 0:wd_])

    x_fm = x_scope.enter_context(nc.sbuf_tensor("x_fm", [128, DC, N], BF16,
                                                is_transpose_dest=True))
    if upto >= 2:
        bd.w(sy, "dsp")
        for tb in range(4):
            bd.dmat(sy, "dtr", x_fm[:, :, tb * 512:(tb + 1) * 512],
                    x_bf[tb * 512:(tb + 1) * 512, :])

    if upto >= 3:
        barrier()
        p3 = ExitStack()

        def sb3(name, shape, dt, **kw):
            return p3.enter_context(nc.sbuf_tensor(name, shape, dt, **kw))

        wb = [sb3("p3_wb%d" % i, [128, DC, 512], BF16) for i in range(3)]
        WBS = ("dgB", "dgC", "dgD")
        tb_c = [sb3("p3_tc%d" % i, [128, 512], BF16) for i in range(4)]
        tb_s = [sb3("p3_ts%d" % i, [128, 512], BF16) for i in range(4)]
        tf = [sb3("p3_tf%d" % i, [128, 512], F32) for i in range(4)]
        tsh = [sb3("p3_tb%d" % i, [128, 512], F32) for i in range(4)]
        SWAPM = [(i + 1 if i % 2 == 0 else i - 1) for i in range(32)]
        spl = [sb3("p3_sp0", [128, N], BF16), sb3("p3_sp1", [128, N], BF16)]
        vb = [sb3("p3_v0", [128, 512], BF16), sb3("p3_v1", [128, 512], BF16)]
        TLS = ("dld2A", "dld2B", "dld2C", "dld2D")
        war = dict(wb=[0, 0, 0], wbi=0, tab=[0] * 4, tf=[0] * 4, tbb=[0] * 4, ub=[0] * 4,
                   psA=[0] * 4, psR=[0, 0], spl=[0, 0], vb=[0, 0])
        ui = 0

        def proj_ii(wap, bias_col, ctab, stab, dstf, is_g, wnm):
            nonlocal ui
            for wc in range(4):
                wpp = war["wbi"] % 3
                war["wbi"] += 1
                gs = WBS[wpp]
                if wc == 0:
                    bd.w(sy, "dgA", wcast_done[wnm])
                if war["wb"][wpp]:
                    bd.w(sy, "pe", war["wb"][wpp])
                bd.dma(sy, gs, wb[wpp][:],
                       wap.rearrange("(ic p) o -> p ic o", p=128)[:, :, wc * 512:(wc + 1) * 512])
                bd.w(t, gs)
                for oc4 in range(4):
                    oc = wc * 4 + oc4
                    osl = slice(oc4 * 128, (oc4 + 1) * 128)
                    spp = oc % 2
                    p1_step()
                    if war["spl"][spp]:
                        bd.w(a if is_g else v, "dspA" if spp == 0 else "dspB")
                    for half in range(2):
                        up = [ui % 4, (ui + 1) % 4]
                        tcs = [2 * half, 2 * half + 1]
                        ui += 2
                        tsls = [slice(tc * 512, (tc + 1) * 512) for tc in tcs]
                        if not is_g:
                            for k2 in range(2):
                                pp = up[k2]
                                if war["tab"][pp]:
                                    bd.w(sy, "dve", war["tab"][pp])
                                bd.dma(sy, TLS[pp], tb_c[pp][:],
                                       ctab[oc * 128:(oc + 1) * 128, tsls[k2]])
                                bd.dma(sy, TLS[pp], tb_s[pp][:],
                                       stab[oc * 128:(oc + 1) * 128, tsls[k2]])
                        for pp in up:
                            if war["psA"][pp]:
                                bd.w(t, "act", war["psA"][pp])
                        for ic in range(DC):
                            for k2 in range(2):
                                mm = t.matmul(psA[up[k2]][:], wb[wpp][:, ic, osl],
                                              x_fm[:, ic, tsls[k2]],
                                              start=(ic == 0), stop=(ic == DC - 1))
                                if ic == DC - 1 and k2 == 1:
                                    pe_mm = bd.inc(mm, "pe")
                        war["wb"][wpp] = pe_mm
                        bd.w(a, "pe", pe_mm)
                        two_units = [(up[k2], tsls[k2]) for k2 in range(2)]
                        for pp, tsl in two_units:
                            if is_g:
                                act_t = bd.inc(a.activation(spl[spp][:, tsl], psA[pp][:], AF.Silu,
                                                            bias=bias_col[:, oc:oc + 1]), "act")
                                war["psA"][pp] = act_t
                                continue
                            if war["tf"][pp]:
                                bd.w(a, "dve", war["tf"][pp])
                            act_t = bd.inc(a.activation(tf[pp][:], psA[pp][:], AF.Identity,
                                                        bias=bias_col[:, oc:oc + 1]), "act")
                            war["psA"][pp] = act_t
                            bd.w(v, TLS[pp])
                            bd.w(v, "act", act_t)
                            if war["tbb"][pp]:
                                bd.w(v, "dve", war["tbb"][pp])
                            d_sh = bd.inc(v.stream_shuffle(tsh[pp][:], tf[pp][:], SWAPM), "dve")
                            bd.w(v, "dve", d_sh)
                            d_u = bd.inc(v.tensor_tensor(tf[pp][:], tf[pp][:], tb_c[pp][:],
                                                         ALU.mult), "dve")
                            d_r = bd.inc(v.tensor_tensor(tsh[pp][:], tsh[pp][:], tb_s[pp][:],
                                                         ALU.mult), "dve")
                            war["tab"][pp] = d_r
                            bd.w(v, "dve", d_r)
                            d_add = bd.inc(v.tensor_tensor(spl[spp][:, tsl], tf[pp][:], tsh[pp][:],
                                                           ALU.add), "dve")
                            war["tf"][pp] = d_add
                            war["tbb"][pp] = d_add
                    bd.w(p, "act" if is_g else "dve")
                    war["spl"][spp] = bd.dma(p, "dspA" if spp == 0 else "dspB",
                                             dstf[oc], spl[spp][:])

        proj_ii(watt["wq"], col["bq"], tabs["cosq"], tabs["sinq"], qf, False, "wq")
        proj_ii(watt["wk"], col["bk"], tabs["cosk"], tabs["sink"], kf, False, "wk")
        proj_ii(watt["wg"], col["bg"], None, None, gf, True, "wg")

        bd.w(t, "act")
        war["psA"] = [0] * 4
        bd.w(sy, "dgA", wcast_done["wv"])
        for wc in range(4):
            wpp = war["wbi"] % 3
            war["wbi"] += 1
            gs = WBS[wpp]
            if war["wb"][wpp]:
                bd.w(sy, "pe", war["wb"][wpp])
            bd.dma(sy, gs, wb[wpp][:],
                   watt["wv"].rearrange("(ic p) o -> p ic o", p=128)[:, :, wc * 512:(wc + 1) * 512])
            bd.w(t, gs)
            for tt in range(NT):
                pp = tt % 4
                vp = tt % 2
                p1_step()
                p1_step()
                if war["psA"][pp]:
                    bd.w(t, "dve", war["psA"][pp])
                for ic in range(DC - 1):
                    t.matmul(psA[pp][:], x_fm[:, ic, tt * 128:(tt + 1) * 128], wb[wpp][:, ic, :],
                             start=(ic == 0), stop=False)
                pe_mm = bd.inc(t.matmul(psA[pp][:], x_fm[:, DC - 1, tt * 128:(tt + 1) * 128],
                                        wb[wpp][:, DC - 1, :], start=False, stop=True), "pe")
                war["wb"][wpp] = pe_mm
                bd.w(v, "pe", pe_mm)
                if war["vb"][vp]:
                    bd.w(v, "dspA" if vp == 0 else "dspB")
                d_v = bd.inc(v.tensor_tensor(vb[vp][:], psA[pp][:],
                                             bv_bc[:, wc * 512:(wc + 1) * 512], ALU.add), "dve")
                war["psA"][pp] = d_v
                bd.w(p, "dve", d_v)
                war["vb"][vp] = bd.dma(p, "dspA" if vp == 0 else "dspB",
                                       v_tm[tt * 128:(tt + 1) * 128,
                                            wc * 512:(wc + 1) * 512], vb[vp][:])
        while p1_state["ji"] < len(p1_jobs):
            p1_step()
        barrier()
        p3.close()

    x_scope.close()
    if upto >= 4:

        p4 = ExitStack()

        def sb4(name, shape, dt, **kw):
            return p4.enter_context(nc.sbuf_tensor(name, shape, dt, **kw))

        kc = sb4("p4_kc", [128, DC, N], BF16)
        vc = sb4("p4_vc", [128, NT, N], BF16)
        qb = [sb4("p4_q0", [128, DC, 512], BF16), sb4("p4_q1", [128, DC, 512], BF16)]
        ssb = [sb4("p4_s%d" % i, [128, 512], BF16) for i in range(4)]
        psS = [psA[0], psA[1], psR[0], psR[1]]
        rtb = [sb4("p4_rt%d" % i, [128, N], BF16) for i in range(4)]
        stats = sb4("p4_st", [128, 4, 6], F32)
        mv = sb4("p4_mv", [128, 4, 2], F32)
        vr = sb4("p4_vr", [128, 4], F32)
        vr2 = sb4("p4_vr2", [128, 4], F32)
        rstd = sb4("p4_rs", [128, 4], F32)
        for sb_ in range(4):
            ks = "kv%d" % sb_
            for dc in range(DC):
                bd.dma(a, ks, kc[:, dc, sb_ * 512:(sb_ + 1) * 512],
                       kf[dc][:, sb_ * 512:(sb_ + 1) * 512])
            for tt in range(4 * sb_, 4 * sb_ + 4):
                bd.dma(a, ks, vc[:, tt, :], v_tm[tt * 128:(tt + 1) * 128, :])
        war_qb = [0, 0]
        war_ssb = [0] * 4
        war_psA4 = [0] * 4
        war_rtb = [0, 0, 0, 0]
        war_psRet = [0, 0]
        spp = 0

        def qload(qg):
            qpp = qg % 2
            ls = "dld2A" if qpp == 0 else "dld2B"
            if war_qb[qpp]:
                bd.w(a, "pe", war_qb[qpp])
            for dc in range(DC):
                bd.dma(a, ls, qb[qpp][:, dc, :], qf[dc][:, qg * 512:(qg + 1) * 512])

        qload(0)
        for qg in range(4):
            qpp = qg % 2
            ls = "dld2A" if qpp == 0 else "dld2B"
            bd.w(t, ls)
            bd.w(t, "kv%d" % qg)
            if qg + 1 < 4:
                qload(qg + 1)
            for h in range(H):
                hb = h % 2
                if war_psRet[hb]:
                    bd.w(t, "dve", war_psRet[hb])
                pend = []

                def emit_ret(pp, st, a_cp, d_dg):
                    bd.w(t, "act", a_cp)
                    if d_dg:
                        bd.w(t, "dve", d_dg)
                    last = war_ssb[pp]
                    for qs in range(4):
                        if 4 * qg + qs < st:
                            continue
                        strt = (st == 0) and (qs in (0, 2))
                        last = bd.inc(t.matmul(ret_ps(qs, hb), ssb[pp][:, qs * 128:(qs + 1) * 128],
                                               vc[:, st, h * DH:(h + 1) * DH],
                                               start=strt, stop=(4 * qg + qs == st)), "pe")
                    war_ssb[pp] = last

                for st in range(4 * qg + 4):
                    pp = spp % 4
                    spp += 1
                    if war_psA4[pp]:
                        bd.w(t, "act", war_psA4[pp])
                    t.matmul(psS[pp][:], kc[:, 2 * h, st * 128:(st + 1) * 128],
                             qb[qpp][:, 2 * h, :], start=True, stop=False)
                    pe_sc = bd.inc(t.matmul(psS[pp][:], kc[:, 2 * h + 1, st * 128:(st + 1) * 128],
                                            qb[qpp][:, 2 * h + 1, :], start=False, stop=True), "pe")
                    war_qb[qpp] = pe_sc
                    bd.w(a, "pe", pe_sc)
                    if war_ssb[pp]:
                        bd.w(a, "pe", war_ssb[pp])
                    a_cp = bd.inc(a.activation(ssb[pp][:], psS[pp][:], AF.Identity), "act")
                    war_psA4[pp] = a_cp
                    d_dg = 0
                    if st >= 4 * qg:
                        j0 = (st - 4 * qg) * 128
                        bd.w(v, "act", a_cp)
                        d_dg = bd.inc(v.tensor_tensor(ssb[pp][:, j0:j0 + 128],
                                                      ssb[pp][:, j0:j0 + 128], c_tri[:],
                                                      ALU.mult), "dve")
                    pend.append((pp, st, a_cp, d_dg))
                    if len(pend) > 2:
                        emit_ret(*pend.pop(0))
                for e in pend:
                    emit_ret(*e)
                bd.w(v, "pe")
                dv = 0
                for qs in range(4):
                    dv = bd.inc(v.bn_stats(stats[:, qs, :], ret_ps(qs, hb)), "dve")
                bd.w(v, "dve", dv)
                for qs in range(4):
                    dv = bd.inc(v.bn_aggr(mv[:, qs, :], stats[:, qs, :]), "dve")
                bd.w(a, "dve", dv)
                at = bd.inc(a.activation(vr[:], mv[:, :, 1], AF.Sqrt, bias=c_eps[:, 0:1]), "act")
                bd.w(v, "act", at)
                dv = bd.inc(v.reciprocal(rstd[:], vr[:]), "dve")
                bd.w(v, "dve", dv)
                dlast = 0
                for qs in range(4):
                    if war_rtb[qs]:
                        bd.w(v, "dspA" if qs % 2 == 0 else "dspB")
                    dlast = bd.inc(v.tensor_scalar(rtb[qs][:, h * DH:(h + 1) * DH],
                                                   ret_ps(qs, hb), mv[:, qs, 0:1],
                                                   rstd[:, qs:qs + 1],
                                                   ALU.subtract, ALU.mult), "dve")
                war_psRet[hb] = dlast
            bd.w(p, "dve")
            for qs in range(4):
                war_rtb[qs] = bd.dma(p, "dspA" if qs % 2 == 0 else "dspB",
                                     retn[(qg * 4 + qs) * 128:(qg * 4 + qs + 1) * 128, :],
                                     rtb[qs][:])
        barrier()
        p4.close()

        p4b = ExitStack()

        def sb4b(name, shape, dt, **kw):
            return p4b.enter_context(nc.sbuf_tensor(name, shape, dt, **kw))

        rfm = [sb4b("p4b_rfm0", [128, DC, 512], BF16, is_transpose_dest=True),
               sb4b("p4b_rfm1", [128, DC, 512], BF16, is_transpose_dest=True)]
        gb4 = [sb4b("p4b_g0", [128, DC, 512], BF16), sb4b("p4b_g1", [128, DC, 512], BF16)]
        wb4 = [sb4b("p4b_wb%d" % i, [128, DC, 256], BF16) for i in range(4)]
        xres = sb4b("p4b_x", [128, 4, D], F32)
        zb = sb4b("p4b_z", [128, 4, D], F32)
        st4 = sb4b("p4b_st", [128, 4, 6], F32)
        mv4 = sb4b("p4b_mv", [128, 2], F32)
        sc4 = sb4b("p4b_sc", [128, 4], F32)
        war_wb4 = [0, 0, 0, 0]
        war_gb4 = [0, 0]
        war_gate_pe = [0, 0]
        war_rfm = [0, 0]
        war_xres = 0
        war_y = 0
        war_psAz = [0] * 4
        tr_tick = [0, 0]
        gl_tick = [0, 0]

        def pf4b(qg):
            pp = qg % 2
            rsl = slice(qg * 512, (qg + 1) * 512)
            if war_rfm[pp]:
                bd.w(sy, "pool", war_rfm[pp])
            if war_gate_pe[pp]:
                bd.w(sy, "pe", war_gate_pe[pp])
            bd.dmat(sy, "dtrA" if pp == 0 else "dtrB", rfm[pp][:], retn[rsl, :])
            tr_tick[pp] = bd.cnt["dtrA" if pp == 0 else "dtrB"]
            if war_gb4[pp]:
                bd.w(sy, "pool", war_gb4[pp])
            for dc in range(DC):
                bd.dma(sy, "dld2A" if pp == 0 else "dld2B", gb4[pp][:, dc, :], gf[dc][:, rsl])
            gl_tick[pp] = bd.cnt["dld2A" if pp == 0 else "dld2B"]

        pf4b(0)
        for qg in range(4):
            pp = qg % 2
            if qg + 1 < 4:
                pf4b(qg + 1)
            if war_xres:
                bd.w(sy, "dve", war_xres)
            for qs in range(4):
                bd.dma(sy, "dldA", xres[:, qs, :],
                       x_in[qg * 512 + qs * 128:qg * 512 + (qs + 1) * 128, :])
            if war_y:
                bd.w(v, "dsp", war_y)
            bd.w(v, "dldA")
            for qs in range(4):
                war_xres = bd.inc(v.tensor_tensor(xres[:, qs, :], xres[:, qs, :], bo_bc[:],
                                                  ALU.add), "dve")
            bd.w(p, "dtrA" if pp == 0 else "dtrB", tr_tick[pp])
            bd.w(p, "dld2A" if pp == 0 else "dld2B", gl_tick[pp])
            glast = 0
            for dc in range(DC):
                dv1 = bd.inc(p.tensor_scalar(rfm[pp][:, dc, :], rfm[pp][:, dc, :],
                                             col["gnw"][:, dc:dc + 1], col["gnb"][:, dc:dc + 1],
                                             ALU.mult, ALU.add), "pool")
                bd.w(p, "pool", dv1)
                if war_gate_pe[pp]:
                    bd.w(p, "pe", war_gate_pe[pp])
                glast = bd.inc(p.tensor_tensor(rfm[pp][:, dc, :], rfm[pp][:, dc, :],
                                               gb4[pp][:, dc, :], ALU.mult), "pool")
            war_rfm[pp] = glast
            war_gb4[pp] = glast
            for wc in range(8):
                wpp = wc % 4
                gs = ("dgB", "dgC", "dgD", "dld2C")[wpp]
                if war_wb4[wpp]:
                    bd.w(p, "pe", war_wb4[wpp])
                bd.dma(p, gs, wb4[wpp][:],
                       watt["wo"].rearrange("(ic p) o -> p ic o", p=128)[:, :, wc * 256:(wc + 1) * 256])
                bd.w(t, gs)
                bd.w(t, "pool", glast)
                for qs in range(4):
                    if war_psAz[qs]:
                        bd.w(t, "dve", war_psAz[qs])
                    for ic in range(DC - 1):
                        t.matmul(psA[qs][:, 0:256], rfm[pp][:, ic, qs * 128:(qs + 1) * 128],
                                 wb4[wpp][:, ic, :], start=(ic == 0), stop=False)
                    pe_mm = bd.inc(t.matmul(psA[qs][:, 0:256],
                                            rfm[pp][:, DC - 1, qs * 128:(qs + 1) * 128],
                                            wb4[wpp][:, DC - 1, :], start=False, stop=True), "pe")
                    war_wb4[wpp] = pe_mm
                    war_gate_pe[pp] = pe_mm
                    bd.w(v, "pe", pe_mm)
                    bd.w(v, "dldA")
                    dz = bd.inc(v.tensor_tensor(zb[:, qs, wc * 256:(wc + 1) * 256],
                                                psA[qs][:, 0:256],
                                                xres[:, qs, wc * 256:(wc + 1) * 256], ALU.add), "dve")
                    war_psAz[qs] = dz
                    war_xres = dz
            for qs in range(4):
                bd.w(v, "dve")
                dv = 0
                for c4 in range(4):
                    dv = bd.inc(v.bn_stats(st4[:, c4, :], zb[:, qs, c4 * 512:(c4 + 1) * 512]), "dve")
                bd.w(v, "dve", dv)
                dv = bd.inc(v.bn_aggr(mv4[:], st4.ap().rearrange("p a b -> p (a b)")), "dve")
                bd.w(a, "dve", dv)
                at = bd.inc(a.activation(sc4[:, 1:2], mv4[:, 1:2], AF.Sqrt, bias=c_eps[:, 0:1]), "act")
                bd.w(v, "act", at)
                dv = bd.inc(v.reciprocal(sc4[:, 2:3], sc4[:, 1:2]), "dve")
                bd.w(v, "dve", dv)
                dy = bd.inc(v.tensor_scalar(zb[:, qs, :], zb[:, qs, :], mv4[:, 0:1], sc4[:, 2:3],
                                            ALU.subtract, ALU.mult), "dve")
                bd.w(p, "dve", dy)
                rr = slice(qg * 512 + qs * 128, qg * 512 + (qs + 1) * 128)
                war_y = bd.dma(p, "dsp", y32[rr, :], zb[:, qs, :])
            bd.w(p, "dsp", war_y)
            bd.dma(p, "dspC", ybf[qg * 512:(qg + 1) * 512, :], y32[qg * 512:(qg + 1) * 512, :])
        barrier()
        p4b.close()

    pers.close()
    if upto >= 5:
        p5 = ExitStack()

        def sb5(name, shape, dt, **kw):
            return p5.enter_context(nc.sbuf_tensor(name, shape, dt, **kw))

        bd.w(sy, "dspC")

        yfm = sb5("p5_yfm", [128, DC, 512], BF16, is_transpose_dest=True)
        wgt = [sb5("p5_wg0", [128, DC, 256], BF16, is_transpose_dest=True),
               sb5("p5_wg1", [128, DC, 256], BF16, is_transpose_dest=True)]
        wut = [sb5("p5_wu0", [128, DC, 256], BF16, is_transpose_dest=True),
               sb5("p5_wu1", [128, DC, 256], BF16, is_transpose_dest=True)]
        h1 = sb5("p5_h1", [128, HC, 512], BF16)
        sg5 = [sb5("p5_sg0", [128, 512], F32), sb5("p5_sg1", [128, 512], F32)]
        wdt = [sb5("p5_wd0", [128, HC // 2, 512], BF16, is_transpose_dest=True),
               sb5("p5_wd1", [128, HC // 2, 512], BF16, is_transpose_dest=True)]
        wd_ap = wbf["d"]
        yres = sb5("p5_y", [128, 4, D], F32)
        z5 = sb5("p5_z", [128, 4, D], F32)
        st5 = sb5("p5_st", [128, 4, 6], F32)
        mv5 = sb5("p5_mv", [128, 2], F32)
        sc5 = sb5("p5_sc", [128, 4], F32)
        war_yfm = 0
        war_wgt = [0, 0]
        war_wdt = [0, 0]
        war_h1_pe = 0
        war_sgA = [0, 0]
        war_psRu = [0, 0]
        war_sg5 = [0, 0]
        war_psAd = [0] * 4
        war_yres = 0
        war_o5 = 0
        for grp in range(4):
            rsl = slice(grp * 512, (grp + 1) * 512)
            if war_yfm:
                bd.w(sy, "pe", war_yfm)
            bd.dmat(sy, "dtr", yfm[:], ybf[rsl, :])
            for hb in range(22):
                wpp = hb % 2
                if war_wgt[wpp]:
                    bd.w(sy, "pe", war_wgt[wpp])
                bd.dmat(sy, "dtr", wgt[wpp][:], wbf["g"][hb * 256:(hb + 1) * 256, :])
                bd.dmat(sy, "dtr", wut[wpp][:], wbf["u"][hb * 256:(hb + 1) * 256, :])
                bd.w(t, "dtr")
                for hc2 in range(2):
                    hc = hb * 2 + hc2
                    pp = hc % 2
                    hsl = slice(hc2 * 128, (hc2 + 1) * 128)
                    if war_sgA[pp]:
                        bd.w(t, "act", war_sgA[pp])
                    if war_psAd[pp]:
                        bd.w(t, "dve", war_psAd[pp])
                    for ic in range(DC - 1):
                        t.matmul(psA[pp][:], wgt[wpp][:, ic, hsl], yfm[:, ic, :],
                                 start=(ic == 0), stop=False)
                    peg = bd.inc(t.matmul(psA[pp][:], wgt[wpp][:, DC - 1, hsl],
                                          yfm[:, DC - 1, :], start=False, stop=True), "pe")
                    if war_psRu[pp]:
                        bd.w(t, "dve", war_psRu[pp])
                    for ic in range(DC - 1):
                        t.matmul(psR[pp][:], wut[wpp][:, ic, hsl], yfm[:, ic, :],
                                 start=(ic == 0), stop=False)
                    peu = bd.inc(t.matmul(psR[pp][:], wut[wpp][:, DC - 1, hsl],
                                          yfm[:, DC - 1, :], start=False, stop=True), "pe")
                    war_wgt[wpp] = peu
                    war_yfm = peu
                    bd.w(a, "pe", peg)
                    if war_sg5[pp]:
                        bd.w(a, "dve", war_sg5[pp])
                    at = bd.inc(a.activation(sg5[pp][:], psA[pp][:], AF.Silu), "act")
                    war_sgA[pp] = at
                    bd.w(v, "act", at)
                    bd.w(v, "pe", peu)
                    if war_h1_pe:
                        bd.w(v, "pe", war_h1_pe)
                    dh = bd.inc(v.tensor_tensor(h1[:, hc, :], sg5[pp][:], psR[pp][:],
                                                ALU.mult), "dve")
                    war_sg5[pp] = dh
                    war_psRu[pp] = dh
            bd.w(t, "act")
            if war_yres:
                bd.w(a, "dve", war_yres)
            if war_o5:
                bd.w(a, "dst", war_o5)
            for ts in range(4):
                bd.dma(a, "dld2A", yres[:, ts, :],
                       y32[grp * 512 + ts * 128:grp * 512 + (ts + 1) * 128, :])
            HHC = HC // 2
            for db in range(4):
                dsl = slice(db * 512, (db + 1) * 512)
                for hcb in range(2):
                    ds_ = "dtrA" if hcb == 0 else "dtrB"
                    if war_wdt[hcb]:
                        bd.w(sy, "pe", war_wdt[hcb])
                    bd.dmat(sy, ds_, wdt[hcb][:],
                            wd_ap[dsl, hcb * 2816:(hcb + 1) * 2816])
                bd.w(t, "dve")
                for hcb in range(2):
                    ds_ = "dtrA" if hcb == 0 else "dtrB"
                    bd.w(t, ds_)
                    pe_half = 0
                    for hc2 in range(HHC):
                        hc = hcb * HHC + hc2
                        for ts in range(4):
                            mm = t.matmul(psA[ts][:], h1[:, hc, ts * 128:(ts + 1) * 128],
                                          wdt[hcb][:, hc2, :],
                                          start=(hc == 0), stop=(hc == HC - 1))
                            if hc2 == HHC - 1 and ts == 3:
                                pe_half = bd.inc(mm, "pe")
                    war_wdt[hcb] = pe_half
                for ts in range(4):
                    bd.w(v, "pe")
                    bd.w(v, "dld2A")
                    dz = bd.inc(v.tensor_tensor(z5[:, ts, dsl], psA[ts][:],
                                                yres[:, ts, dsl], ALU.add), "dve")
                    war_psAd[ts] = dz
                    war_yres = dz
            for ts in range(4):
                bd.w(v, "dve")
                dv = 0
                for c4 in range(4):
                    dv = bd.inc(v.bn_stats(st5[:, c4, :], z5[:, ts, c4 * 512:(c4 + 1) * 512]), "dve")
                bd.w(v, "dve", dv)
                dv = bd.inc(v.bn_aggr(mv5[:], st5.ap().rearrange("p a b -> p (a b)")), "dve")
                bd.w(a, "dve", dv)
                at = bd.inc(a.activation(sc5[:, 1:2], mv5[:, 1:2], AF.Sqrt, bias=c_eps[:, 0:1]), "act")
                bd.w(v, "act", at)
                dv = bd.inc(v.reciprocal(sc5[:, 2:3], sc5[:, 1:2]), "dve")
                bd.w(v, "dve", dv)
                if war_o5:
                    bd.w(v, "dst", war_o5)
                dy = bd.inc(v.tensor_scalar(yres[:, ts, :], z5[:, ts, :], mv5[:, 0:1], sc5[:, 2:3],
                                            ALU.subtract, ALU.mult), "dve")
                bd.w(p, "dve", dy)
                rr = slice(grp * 512 + ts * 128, grp * 512 + (ts + 1) * 128)
                p.dma_start(out=out[rr, :], in_=yres[:, ts, :]).then_inc(bd.sem["dst"], 16)
                bd.cnt["dst"] += 16
                war_o5 = bd.cnt["dst"]
        p5.close()

    bd.w(p, "dst")
    bd.w(p, "dsp")
    bd.w(sy, "dst")
    bd.close()
    return nc


def _in_maps(inputs):
    tb = _host_tables()
    f32c = lambda k: np.ascontiguousarray(inputs[k], dtype=np.float32)
    common = dict(
        wq=f32c("L0_attn_w_q"), wk=f32c("L0_attn_w_k"), wv=f32c("L0_attn_w_v"),
        wg=f32c("L0_attn_w_g"), wo=f32c("L0_attn_w_o"),
        bq_r=np.asarray(inputs["L0_attn_b_q"], np.float32).reshape(1, D),
        bk_r=np.asarray(inputs["L0_attn_b_k"], np.float32).reshape(1, D),
        bg_r=np.asarray(inputs["L0_attn_b_g"], np.float32).reshape(1, D),
        bv_r=np.asarray(inputs["L0_attn_b_v"], np.float32).reshape(1, D),
        bo_r=np.asarray(inputs["L0_attn_b_o"], np.float32).reshape(1, D),
        gnw_r=np.asarray(inputs["L0_attn_gn_weight"], np.float32).reshape(1, D),
        gnb_r=np.asarray(inputs["L0_attn_gn_bias"], np.float32).reshape(1, D),
        sd_r=np.asarray(inputs["L0_ffn_s_down"], np.float32).reshape(1, D),
        sg_r=np.asarray(inputs["L0_ffn_s_gate"], np.float32).reshape(1, HID),
        su_r=np.asarray(inputs["L0_ffn_s_up"], np.float32).reshape(1, HID),
        wqg=np.ascontiguousarray(inputs["L0_ffn_wq_gate"], dtype=np.int32),
        wqu=np.ascontiguousarray(inputs["L0_ffn_wq_up"], dtype=np.int32),
        wqd=np.ascontiguousarray(inputs["L0_ffn_wq_down"], dtype=np.int32),
        cosq=tb["cosq"], sinq=tb["sinq"], cosk=tb["cosk"], sink=tb["sink"],
        trimask=tb["trimask"], perm=tb["perm"], ones1=tb["ones1"], onesf=tb["onesf"],
    )
    x_full = np.asarray(inputs["L0_attn_x"], np.float32)
    return [dict(common, x=np.ascontiguousarray(x_full[c])) for c in range(B)]


def kernel(**inputs):
    in_maps = _in_maps(inputs)
    nc = build()
    res = run_bass_kernel_spmd(nc, in_maps, core_ids=list(range(B)))
    return np.stack([res.results[c]["out"] for c in range(B)], axis=0)


# version 16
# speedup vs baseline: 1.0620x; 1.0620x over previous
import math
from contextlib import ExitStack
import numpy as np
import ml_dtypes
import concourse.bass as bass
import concourse.mybir as mybir
import concourse.bass_utils as _bu
from concourse.bass_utils import run_bass_kernel_spmd

if not getattr(_bu, "_ldwopt_patched", False):
    _orig_run_command = _bu.run_command

    def _run_command_ldw(argv, **kw):
        argv = ["--enable-ldw-opt=true" if a == "--enable-ldw-opt=false" else a
                for a in argv]
        if any(a.startswith("--enable-ldw-opt") for a in argv):
            argv = argv[:-2] + ["--enable-separate-load-and-compute=true"] + argv[-2:]
        return _orig_run_command(argv, **kw)

    _bu.run_command = _run_command_ldw
    _bu._ldwopt_patched = True

F32 = mybir.dt.float32
BF16 = mybir.dt.bfloat16
I32 = mybir.dt.int32
AF = mybir.ActivationFunctionType
ALU = mybir.AluOpType

B, N, D, H, DH, HID = 8, 2048, 2048, 8, 256, 5632
NT = N // 128
DC = D // 128
HC = HID // 128
EPS = 1e-5

_TAB = {}


def _host_tables():
    if _TAB:
        return _TAB
    f8 = np.float64
    xs = np.linspace(0.0, 1.0, DH // 2)
    thetas = np.repeat(1.0 / (10000.0 ** xs), 2)
    pos = np.arange(N, dtype=f8)
    ang = pos[:, None] * thetas[None, :]
    cosT = np.cos(ang).T
    sinT = np.sin(ang).T
    lgam = np.log(1.0 - np.exp(np.linspace(math.log(1 / 32), math.log(1 / 512), H)))
    gq = np.exp(pos[None, :] * lgam[:, None])
    gk = np.exp(-pos[None, :] * lgam[:, None]) / 16.0
    cosq = np.empty((D, N), np.float32); sinq = np.empty((D, N), np.float32)
    cosk = np.empty((D, N), np.float32); sink = np.empty((D, N), np.float32)
    sgn = np.where(np.arange(DH) % 2 == 0, -1.0, 1.0)[:, None]
    for h in range(H):
        sl = slice(h * DH, (h + 1) * DH)
        cosq[sl] = (cosT * gq[h][None, :]).astype(np.float32)
        sinq[sl] = (sgn * sinT * gq[h][None, :]).astype(np.float32)
        cosk[sl] = (cosT * gk[h][None, :]).astype(np.float32)
        sink[sl] = (sgn * sinT * gk[h][None, :]).astype(np.float32)
    tri = (np.arange(128)[None, :] >= np.arange(128)[:, None]).astype(np.float32)
    perm = np.zeros((128, 128), np.float32)
    for i in range(64):
        perm[2 * i, 2 * i + 1] = 1.0
        perm[2 * i + 1, 2 * i] = -1.0
    bfc = ml_dtypes.bfloat16
    _TAB.update(dict(cosq=cosq.astype(bfc), sinq=sinq.astype(bfc),
                     cosk=cosk.astype(bfc), sink=sink.astype(bfc), trimask=tri,
                     perm=perm.astype(ml_dtypes.bfloat16),
                     ones1=np.ones((1, 128), ml_dtypes.bfloat16),
                     onesf=np.ones((1, 1), np.float32)))
    return _TAB


class Bld:
    def __init__(self, nc):
        self.nc = nc
        self.sem = {}
        self.cnt = {}
        self.hw = {}
        self._stack = []

    def mksem(self, name):
        cm = self.nc.semaphore(name)
        self.sem[name] = cm.__enter__()
        self._stack.append(cm)
        self.cnt[name] = 0

    def close(self):
        for cm in reversed(self._stack):
            cm.__exit__(None, None, None)

    def inc(self, instr, s, step=1):
        self.cnt[s] += step
        instr.then_inc(self.sem[s], step)
        return self.cnt[s]

    def dma(self, eng, s, out, in_):
        eng.dma_start(out=out, in_=in_).then_inc(self.sem[s], 16)
        self.cnt[s] += 16
        return self.cnt[s]

    def dmat(self, eng, s, out, in_):
        eng.dma_start_transpose(out=out, in_=in_).then_inc(self.sem[s], 16)
        self.cnt[s] += 16
        return self.cnt[s]

    def w(self, eng, s, val=None):
        if val is None:
            val = self.cnt[s]
        if val <= 0:
            return
        k = (id(eng), s)
        if self.hw.get(k, -1) >= val:
            return
        eng.wait_ge(self.sem[s], val)
        self.hw[k] = val


def build(upto=5, dbg=()):
    nc = bass.Bass("TRN2", target_bir_lowering=False, debug=False)
    t, a, v, p, sy = nc.tensor, nc.scalar, nc.vector, nc.gpsimd, nc.sync

    def dram(name, shape, dt, kind):
        if name in dbg and kind == "Internal":
            kind = "ExternalOutput"
        return nc.dram_tensor(name, shape, dt, kind=kind).ap()

    x_in = dram("x", [N, D], F32, "ExternalInput")
    W = {nm: dram(nm, [D, D], F32, "ExternalInput") for nm in ("wq", "wk", "wv", "wg", "wo")}
    row_in = {nm: dram(nm, [1, D], F32, "ExternalInput")
              for nm in ("bq_r", "bk_r", "bg_r", "bv_r", "bo_r", "gnw_r", "gnb_r", "sd_r")}
    row_in["sg_r"] = dram("sg_r", [1, HID], F32, "ExternalInput")
    row_in["su_r"] = dram("su_r", [1, HID], F32, "ExternalInput")
    wq_int = {"g": dram("wqg", [HID, D], I32, "ExternalInput"),
              "u": dram("wqu", [HID, D], I32, "ExternalInput"),
              "d": dram("wqd", [D, HID], I32, "ExternalInput")}
    tabs = {nm: dram(nm, [D, N], BF16, "ExternalInput") for nm in ("cosq", "sinq", "cosk", "sink")}
    trimask_in = dram("trimask", [128, 128], F32, "ExternalInput")
    perm_in = dram("perm", [128, 128], BF16, "ExternalInput")
    ones_in = dram("ones1", [1, 128], BF16, "ExternalInput")
    onesf_in = dram("onesf", [1, 1], F32, "ExternalInput")

    out = dram("out", [N, D], F32, "ExternalOutput")
    wbf = {"g": dram("wg_bf", [HID, D], BF16, "Internal"),
           "u": dram("wu_bf", [HID, D], BF16, "Internal"),
           "d": dram("wd_bf", [D, HID], BF16, "Internal")}
    x_bf = dram("x_bf", [N, D], BF16, "Internal")
    watt = {nm: dram("a" + nm + "_bf", [D, D], BF16, "Internal") for nm in ("wq", "wk", "wv", "wg", "wo")}
    qf = dram("qf", [DC, 128, N], BF16, "Internal")
    kf = dram("kf", [DC, 128, N], BF16, "Internal")
    gf = dram("gf", [DC, 128, N], BF16, "Internal")
    v_tm = dram("v_tm", [N, D], BF16, "Internal")
    retn = dram("retn", [N, D], BF16, "Internal")
    y32 = dram("y32", [N, D], F32, "Internal")
    ybf = dram("ybf", [N, D], BF16, "Internal")

    pers = ExitStack()

    def sbp(name, shape, dt, **kw):
        return pers.enter_context(nc.sbuf_tensor(name, shape, dt, **kw))

    c_eps = nc.alloc_sbuf_tensor("c_eps", [128, 1], F32)
    c_tri = sbp("c_tri", [128, 128], F32)
    c_perm = sbp("c_perm", [128, 128], BF16)
    c_ones = sbp("c_ones", [1, 128], BF16)
    c_onef = sbp("c_onef", [1, 1], F32)
    col = {nm: sbp("c_" + nm, [128, DC], F32) for nm in ("bq", "bk", "bg", "gnw", "gnb", "sd")}
    col["sg"] = sbp("c_sg", [128, HC], F32)
    col["su"] = sbp("c_su", [128, HC], F32)
    colm = {"sg": sbp("cm_sg", [128, HC], F32), "su": sbp("cm_su", [128, HC], F32),
            "sd": sbp("cm_sd", [128, DC], F32)}
    bv_bc = sbp("c_bv_bc", [128, D], F32)
    bo_bc = sbp("c_bo_bc", [128, D], F32)
    p0 = ExitStack()
    rows = {nm: p0.enter_context(nc.sbuf_tensor(
        "r_" + nm, [1, HID if nm in ("sg_r", "su_r") else D], F32)) for nm in row_in}
    onesf_row = p0.enter_context(nc.sbuf_tensor("c_onesf_row", [1, 128], F32))

    psA = [nc.alloc_psum_tensor("psA%d" % i, [128, 512], F32) for i in range(4)]
    psR = [nc.alloc_psum_tensor("psR0", [128, 512], F32),
           nc.alloc_psum_tensor("psR1", [128, 512], F32)]
    psm_scope = ExitStack()
    psSm = psm_scope.enter_context(nc.psum_tensor("psSm", [128, 64], F32))

    def ret_ps(qs, hb=0):
        if hb == 0:
            return psA[2 + qs // 2][:, (qs % 2) * 256:(qs % 2) * 256 + 256]
        return psRetB[2 + qs // 2][:, (qs % 2) * 256:(qs % 2) * 256 + 256]

    bd = Bld(nc)
    for s in ("pe", "act", "dve", "pool",
              "dcst", "dldA", "dldB", "dld2A", "dld2B", "dld2C", "dld2D", "dgA", "dgB", "dgC", "dgD", "dgE",
              "dsp", "dspA", "dspB", "dspC", "dspD", "p1a", "p1b", "p1c", "p1d", "kv0", "kv1", "kv2", "kv3", "dtr", "dtrA", "dtrB", "dst"):
        bd.mksem(s)
    SEMS = list(bd.sem.keys())
    ENG = [t, a, v, p, sy]

    def barrier():
        for e in ENG:
            for s in SEMS:
                bd.w(e, s)

    wcast_done = {}
    if upto >= 2:
        for tb in range(4):
            bd.dma(p, "dsp", x_bf[tb * 512:(tb + 1) * 512, :], x_in[tb * 512:(tb + 1) * 512, :])
        for nm in ("wq", "wk", "wg", "wv", "wo"):
            for ch in range(4):
                bd.dma(p, "dgA", watt[nm][ch * 512:(ch + 1) * 512, :],
                       W[nm][ch * 512:(ch + 1) * 512, :])
            wcast_done[nm] = bd.cnt["dgA"]
    for dst, src in [(c_tri[:], trimask_in[:]), (c_perm[:], perm_in[:]),
                     (c_ones[:], ones_in[:]), (c_onef[:], onesf_in[:])] + \
                    [(rows[nm][:], row_in[nm][:]) for nm in rows]:
        bd.dma(sy, "dcst", dst, src)
    bd.w(t, "dcst")
    bd.w(v, "dcst")
    for rnm, cdst, nch in [("bq_r", col["bq"], DC), ("bk_r", col["bk"], DC),
                           ("bg_r", col["bg"], DC), ("gnw_r", col["gnw"], DC),
                           ("gnb_r", col["gnb"], DC), ("sd_r", col["sd"], DC),
                           ("sg_r", col["sg"], HC), ("su_r", col["su"], HC)]:
        pe_t = 0
        for c in range(nch):
            mm = t.matmul(psSm[:, c:c + 1], rows[rnm][0:1, c * 128:(c + 1) * 128],
                          c_onef[0:1, 0:1], start=(c == 0), stop=(c == nch - 1))
            if c == nch - 1:
                pe_t = bd.inc(mm, "pe")
        bd.w(a, "pe", pe_t)
        act_t = bd.inc(a.activation(cdst[:], psSm[:, 0:nch], AF.Identity), "act")
        bd.w(t, "act", act_t)
    bd.w(v, "act")
    for nm in ("sg", "su", "sd"):
        src = col[nm] if nm != "sd" else col["sd"]
        bd.inc(v.tensor_scalar(colm[nm][:], src[:], -128.0, None, ALU.mult), "dve")
    dvo = bd.inc(v.memset(onesf_row[:], 1.0), "dve")
    bd.inc(v.memset(c_eps[:], EPS), "dve")
    bd.w(t, "dve", dvo)
    for rnm, bdst in (("bv_r", bv_bc), ("bo_r", bo_bc)):
        for j in range(4):
            pe_t = bd.inc(t.matmul(psA[0][:], onesf_row[:], rows[rnm][0:1, j * 512:(j + 1) * 512],
                                   start=True, stop=True), "pe")
            bd.w(a, "pe", pe_t)
            act_t = bd.inc(a.activation(bdst[:, j * 512:(j + 1) * 512], psA[0][:], AF.Identity),
                           "act")
            bd.w(t, "act", act_t)
    barrier()
    p0.close()
    psm_scope.close()
    psRetB = [None, None,
              nc.alloc_psum_tensor("psRB2", [128, 512], F32),
              nc.alloc_psum_tensor("psRB3", [128, 512], F32)]

    x_scope = ExitStack()

    def sb(name, shape, dt, **kw):
        return x_scope.enter_context(nc.sbuf_tensor(name, shape, dt, **kw))

    p1_jobs = []
    for m in ("g", "u"):
        for c in range(HC):
            p1_jobs.append((m, c, 0, D))
    for c in range(DC):
        for off, wd_ in ((0, 2048), (2048, 2048), (4096, 1536)):
            p1_jobs.append(("d", c, off, wd_))
    p1_state = dict(ji=0, li=0, war_i=[0, 0, 0, 0], war_o=[0, 0])
    i1 = [sb("p1_i%d" % i, [128, 2048], BF16) for i in range(4)]
    o1 = [sb("p1_o0", [128, 2048], BF16), sb("p1_o1", [128, 2048], BF16)]
    P1LS = ("p1a", "p1b", "p1c", "p1d")
    bd.w(a, "dve")

    def p1_load():
        li = p1_state["li"]
        if li >= len(p1_jobs):
            return
        p1_state["li"] += 1
        m, ch, off, wd_ = p1_jobs[li]
        lp = li % 4
        if p1_state["war_i"][lp]:
            bd.w(p, "act", p1_state["war_i"][lp])
        bd.dma(p, P1LS[lp], i1[lp][:, 0:wd_], wq_int[m][ch * 128:(ch + 1) * 128, off:off + wd_])

    def p1_step():
        while p1_state["li"] < min(p1_state["ji"] + 4, len(p1_jobs)):
            p1_load()
        ji = p1_state["ji"]
        if ji >= len(p1_jobs):
            return
        p1_state["ji"] += 1
        m, ch, off, wd_ = p1_jobs[ji]
        lp = ji % 4
        pp = ji % 2
        bd.w(a, P1LS[lp])
        ss_ = "dspC" if pp == 0 else "dspD"
        if p1_state["war_o"][pp]:
            bd.w(a, ss_)
        sc, scm = (col["sg"], colm["sg"]) if m == "g" else \
                  (col["su"], colm["su"]) if m == "u" else (col["sd"], colm["sd"])
        act_t = bd.inc(a.activation(o1[pp][:, 0:wd_], i1[lp][:, 0:wd_], AF.Identity,
                                    bias=scm[:, ch:ch + 1], scale=sc[:, ch:ch + 1]), "act")
        p1_state["war_i"][lp] = act_t
        bd.w(p, "act", act_t)
        p1_state["war_o"][pp] = bd.dma(p, ss_, wbf[m][ch * 128:(ch + 1) * 128, off:off + wd_],
                                       o1[pp][:, 0:wd_])

    x_fm = x_scope.enter_context(nc.sbuf_tensor("x_fm", [128, DC, N], BF16,
                                                is_transpose_dest=True))
    if upto >= 2:
        bd.w(sy, "dsp")
        for tb in range(4):
            bd.dmat(sy, "dtr", x_fm[:, :, tb * 512:(tb + 1) * 512],
                    x_bf[tb * 512:(tb + 1) * 512, :])

    if upto >= 3:
        barrier()
        p3 = ExitStack()

        def sb3(name, shape, dt, **kw):
            return p3.enter_context(nc.sbuf_tensor(name, shape, dt, **kw))

        wb = [sb3("p3_wb%d" % i, [128, DC, 512], BF16) for i in range(4)]
        WBS = ("dgB", "dgC", "dgD", "dgE")
        tb_c = [sb3("p3_tc%d" % i, [128, 512], BF16) for i in range(4)]
        tb_s = [sb3("p3_ts%d" % i, [128, 512], BF16) for i in range(4)]
        tf = [sb3("p3_tf%d" % i, [128, 512], F32) for i in range(4)]
        tsh = [sb3("p3_tb%d" % i, [128, 512], F32) for i in range(4)]
        SWAPM = [(i + 1 if i % 2 == 0 else i - 1) for i in range(32)]
        spl = [sb3("p3_sp0", [128, N], BF16), sb3("p3_sp1", [128, N], BF16)]
        vb = [sb3("p3_v0", [128, 512], BF16), sb3("p3_v1", [128, 512], BF16)]
        TLS = ("dld2A", "dld2B", "dld2C", "dld2D")
        war = dict(wb=[0, 0, 0, 0], wbi=0, tab=[0] * 4, tf=[0] * 4, tbb=[0] * 4, ub=[0] * 4,
                   psA=[0] * 4, psR=[0, 0], spl=[0, 0], vb=[0, 0])
        ui = 0

        def proj_ii(wap, bias_col, ctab, stab, dstf, is_g, wnm):
            nonlocal ui
            for wc in range(4):
                wpp = war["wbi"] % 4
                war["wbi"] += 1
                gs = WBS[wpp]
                if wc == 0:
                    bd.w(sy, "dgA", wcast_done[wnm])
                if war["wb"][wpp]:
                    bd.w(sy, "pe", war["wb"][wpp])
                bd.dma(sy, gs, wb[wpp][:],
                       wap.rearrange("(ic p) o -> p ic o", p=128)[:, :, wc * 512:(wc + 1) * 512])
                bd.w(t, gs)
                for oc4 in range(4):
                    oc = wc * 4 + oc4
                    osl = slice(oc4 * 128, (oc4 + 1) * 128)
                    spp = oc % 2
                    p1_step()
                    if war["spl"][spp]:
                        bd.w(a if is_g else v, "dspA" if spp == 0 else "dspB")
                    for half in range(2):
                        up = [ui % 4, (ui + 1) % 4]
                        tcs = [2 * half, 2 * half + 1]
                        ui += 2
                        tsls = [slice(tc * 512, (tc + 1) * 512) for tc in tcs]
                        if not is_g:
                            for k2 in range(2):
                                pp = up[k2]
                                if war["tab"][pp]:
                                    bd.w(a, "dve", war["tab"][pp])
                                bd.dma(a, TLS[pp], tb_c[pp][:],
                                       ctab[oc * 128:(oc + 1) * 128, tsls[k2]])
                                bd.dma(a, TLS[pp], tb_s[pp][:],
                                       stab[oc * 128:(oc + 1) * 128, tsls[k2]])
                        for pp in up:
                            if war["psA"][pp]:
                                bd.w(t, "act", war["psA"][pp])
                        for ic in range(DC):
                            for k2 in range(2):
                                mm = t.matmul(psA[up[k2]][:], wb[wpp][:, ic, osl],
                                              x_fm[:, ic, tsls[k2]],
                                              start=(ic == 0), stop=(ic == DC - 1))
                                if ic == DC - 1 and k2 == 1:
                                    pe_mm = bd.inc(mm, "pe")
                        war["wb"][wpp] = pe_mm
                        bd.w(a, "pe", pe_mm)
                        two_units = [(up[k2], tsls[k2]) for k2 in range(2)]
                        for pp, tsl in two_units:
                            if is_g:
                                act_t = bd.inc(a.activation(spl[spp][:, tsl], psA[pp][:], AF.Silu,
                                                            bias=bias_col[:, oc:oc + 1]), "act")
                                war["psA"][pp] = act_t
                                continue
                            if war["tf"][pp]:
                                bd.w(a, "dve", war["tf"][pp])
                            act_t = bd.inc(a.activation(tf[pp][:], psA[pp][:], AF.Identity,
                                                        bias=bias_col[:, oc:oc + 1]), "act")
                            war["psA"][pp] = act_t
                            bd.w(v, TLS[pp])
                            bd.w(v, "act", act_t)
                            if war["tbb"][pp]:
                                bd.w(v, "dve", war["tbb"][pp])
                            d_sh = bd.inc(v.stream_shuffle(tsh[pp][:], tf[pp][:], SWAPM), "dve")
                            bd.w(v, "dve", d_sh)
                            d_u = bd.inc(v.tensor_tensor(tf[pp][:], tf[pp][:], tb_c[pp][:],
                                                         ALU.mult), "dve")
                            d_r = bd.inc(v.tensor_tensor(tsh[pp][:], tsh[pp][:], tb_s[pp][:],
                                                         ALU.mult), "dve")
                            war["tab"][pp] = d_r
                            bd.w(v, "dve", d_r)
                            d_add = bd.inc(v.tensor_tensor(spl[spp][:, tsl], tf[pp][:], tsh[pp][:],
                                                           ALU.add), "dve")
                            war["tf"][pp] = d_add
                            war["tbb"][pp] = d_add
                    bd.w(p, "act" if is_g else "dve")
                    war["spl"][spp] = bd.dma(p, "dspA" if spp == 0 else "dspB",
                                             dstf[oc], spl[spp][:])

        proj_ii(watt["wq"], col["bq"], tabs["cosq"], tabs["sinq"], qf, False, "wq")
        proj_ii(watt["wk"], col["bk"], tabs["cosk"], tabs["sink"], kf, False, "wk")
        proj_ii(watt["wg"], col["bg"], None, None, gf, True, "wg")

        bd.w(t, "act")
        war["psA"] = [0] * 4
        bd.w(sy, "dgA", wcast_done["wv"])
        for wc in range(4):
            wpp = war["wbi"] % 4
            war["wbi"] += 1
            gs = WBS[wpp]
            if war["wb"][wpp]:
                bd.w(sy, "pe", war["wb"][wpp])
            bd.dma(sy, gs, wb[wpp][:],
                   watt["wv"].rearrange("(ic p) o -> p ic o", p=128)[:, :, wc * 512:(wc + 1) * 512])
            bd.w(t, gs)
            for tt in range(NT):
                pp = tt % 4
                vp = tt % 2
                p1_step()
                p1_step()
                if war["psA"][pp]:
                    bd.w(t, "dve", war["psA"][pp])
                for ic in range(DC - 1):
                    t.matmul(psA[pp][:], x_fm[:, ic, tt * 128:(tt + 1) * 128], wb[wpp][:, ic, :],
                             start=(ic == 0), stop=False)
                pe_mm = bd.inc(t.matmul(psA[pp][:], x_fm[:, DC - 1, tt * 128:(tt + 1) * 128],
                                        wb[wpp][:, DC - 1, :], start=False, stop=True), "pe")
                war["wb"][wpp] = pe_mm
                bd.w(v, "pe", pe_mm)
                if war["vb"][vp]:
                    bd.w(v, "dspA" if vp == 0 else "dspB")
                d_v = bd.inc(v.tensor_tensor(vb[vp][:], psA[pp][:],
                                             bv_bc[:, wc * 512:(wc + 1) * 512], ALU.add), "dve")
                war["psA"][pp] = d_v
                bd.w(p, "dve", d_v)
                war["vb"][vp] = bd.dma(p, "dspA" if vp == 0 else "dspB",
                                       v_tm[tt * 128:(tt + 1) * 128,
                                            wc * 512:(wc + 1) * 512], vb[vp][:])
        while p1_state["ji"] < len(p1_jobs):
            p1_step()
        barrier()
        p3.close()

    x_scope.close()
    if upto >= 4:

        p4 = ExitStack()

        def sb4(name, shape, dt, **kw):
            return p4.enter_context(nc.sbuf_tensor(name, shape, dt, **kw))

        kc = sb4("p4_kc", [128, DC, N], BF16)
        vc = sb4("p4_vc", [128, NT, N], BF16)
        qb = [sb4("p4_q0", [128, DC, 512], BF16), sb4("p4_q1", [128, DC, 512], BF16)]
        ssb = [sb4("p4_s%d" % i, [128, 512], BF16) for i in range(4)]
        psS = [psA[0], psA[1], psR[0], psR[1]]
        rtb = [sb4("p4_rt%d" % i, [128, N], BF16) for i in range(4)]
        stats = sb4("p4_st", [128, 4, 6], F32)
        mv = sb4("p4_mv", [128, 4, 2], F32)
        vr = sb4("p4_vr", [128, 4], F32)
        vr2 = sb4("p4_vr2", [128, 4], F32)
        rstd = sb4("p4_rs", [128, 4], F32)
        for sb_ in range(4):
            ks = "kv%d" % sb_
            for dc in range(DC):
                bd.dma(a, ks, kc[:, dc, sb_ * 512:(sb_ + 1) * 512],
                       kf[dc][:, sb_ * 512:(sb_ + 1) * 512])
            for tt in range(4 * sb_, 4 * sb_ + 4):
                bd.dma(a, ks, vc[:, tt, :], v_tm[tt * 128:(tt + 1) * 128, :])
        war_qb = [0, 0]
        war_ssb = [0] * 4
        war_psA4 = [0] * 4
        war_rtb = [0, 0, 0, 0]
        war_psRet = [0, 0]
        spp = 0

        def qload(qg):
            qpp = qg % 2
            ls = "dld2A" if qpp == 0 else "dld2B"
            if war_qb[qpp]:
                bd.w(a, "pe", war_qb[qpp])
            for dc in range(DC):
                bd.dma(a, ls, qb[qpp][:, dc, :], qf[dc][:, qg * 512:(qg + 1) * 512])

        qload(0)
        for qg in range(4):
            qpp = qg % 2
            ls = "dld2A" if qpp == 0 else "dld2B"
            bd.w(t, ls)
            bd.w(t, "kv%d" % qg)
            if qg + 1 < 4:
                qload(qg + 1)
            for h in range(H):
                hb = h % 2
                if war_psRet[hb]:
                    bd.w(t, "dve", war_psRet[hb])
                pend = []

                def emit_ret(pp, st, a_cp, d_dg):
                    bd.w(t, "act", a_cp)
                    if d_dg:
                        bd.w(t, "dve", d_dg)
                    last = war_ssb[pp]
                    for qs in range(4):
                        if 4 * qg + qs < st:
                            continue
                        strt = (st == 0) and (qs in (0, 2))
                        last = bd.inc(t.matmul(ret_ps(qs, hb), ssb[pp][:, qs * 128:(qs + 1) * 128],
                                               vc[:, st, h * DH:(h + 1) * DH],
                                               start=strt, stop=(4 * qg + qs == st)), "pe")
                    war_ssb[pp] = last

                for st in range(4 * qg + 4):
                    pp = spp % 4
                    spp += 1
                    if war_psA4[pp]:
                        bd.w(t, "act", war_psA4[pp])
                    t.matmul(psS[pp][:], kc[:, 2 * h, st * 128:(st + 1) * 128],
                             qb[qpp][:, 2 * h, :], start=True, stop=False)
                    pe_sc = bd.inc(t.matmul(psS[pp][:], kc[:, 2 * h + 1, st * 128:(st + 1) * 128],
                                            qb[qpp][:, 2 * h + 1, :], start=False, stop=True), "pe")
                    war_qb[qpp] = pe_sc
                    bd.w(a, "pe", pe_sc)
                    if war_ssb[pp]:
                        bd.w(a, "pe", war_ssb[pp])
                    a_cp = bd.inc(a.activation(ssb[pp][:], psS[pp][:], AF.Identity), "act")
                    war_psA4[pp] = a_cp
                    d_dg = 0
                    if st >= 4 * qg:
                        j0 = (st - 4 * qg) * 128
                        bd.w(v, "act", a_cp)
                        d_dg = bd.inc(v.tensor_tensor(ssb[pp][:, j0:j0 + 128],
                                                      ssb[pp][:, j0:j0 + 128], c_tri[:],
                                                      ALU.mult), "dve")
                    pend.append((pp, st, a_cp, d_dg))
                    if len(pend) > 2:
                        emit_ret(*pend.pop(0))
                for e in pend:
                    emit_ret(*e)
                bd.w(v, "pe")
                dv = 0
                for qs in range(4):
                    dv = bd.inc(v.bn_stats(stats[:, qs, :], ret_ps(qs, hb)), "dve")
                bd.w(v, "dve", dv)
                for qs in range(4):
                    dv = bd.inc(v.bn_aggr(mv[:, qs, :], stats[:, qs, :]), "dve")
                bd.w(a, "dve", dv)
                at = bd.inc(a.activation(vr[:], mv[:, :, 1], AF.Sqrt, bias=c_eps[:, 0:1]), "act")
                bd.w(v, "act", at)
                dv = bd.inc(v.reciprocal(rstd[:], vr[:]), "dve")
                bd.w(v, "dve", dv)
                dlast = 0
                for qs in range(4):
                    if war_rtb[qs]:
                        bd.w(v, "dspA" if qs % 2 == 0 else "dspB")
                    dlast = bd.inc(v.tensor_scalar(rtb[qs][:, h * DH:(h + 1) * DH],
                                                   ret_ps(qs, hb), mv[:, qs, 0:1],
                                                   rstd[:, qs:qs + 1],
                                                   ALU.subtract, ALU.mult), "dve")
                war_psRet[hb] = dlast
            bd.w(p, "dve")
            for qs in range(4):
                war_rtb[qs] = bd.dma(p, "dspA" if qs % 2 == 0 else "dspB",
                                     retn[(qg * 4 + qs) * 128:(qg * 4 + qs + 1) * 128, :],
                                     rtb[qs][:])
        barrier()
        p4.close()

        p4b = ExitStack()

        def sb4b(name, shape, dt, **kw):
            return p4b.enter_context(nc.sbuf_tensor(name, shape, dt, **kw))

        rfm = [sb4b("p4b_rfm0", [128, DC, 512], BF16, is_transpose_dest=True),
               sb4b("p4b_rfm1", [128, DC, 512], BF16, is_transpose_dest=True)]
        gb4 = [sb4b("p4b_g0", [128, DC, 512], BF16), sb4b("p4b_g1", [128, DC, 512], BF16)]
        wb4 = [sb4b("p4b_wb%d" % i, [128, DC, 256], BF16) for i in range(4)]
        xres = sb4b("p4b_x", [128, 4, D], F32)
        zb = sb4b("p4b_z", [128, 4, D], F32)
        st4 = sb4b("p4b_st", [128, 4, 6], F32)
        mv4 = sb4b("p4b_mv", [128, 2], F32)
        sc4 = sb4b("p4b_sc", [128, 4], F32)
        war_wb4 = [0, 0, 0, 0]
        war_gb4 = [0, 0]
        war_gate_pe = [0, 0]
        war_rfm = [0, 0]
        war_xres = 0
        war_y = 0
        war_psAz = [0] * 4
        tr_tick = [0, 0]
        gl_tick = [0, 0]

        def pf4b(qg):
            pp = qg % 2
            rsl = slice(qg * 512, (qg + 1) * 512)
            if war_rfm[pp]:
                bd.w(sy, "pool", war_rfm[pp])
            if war_gate_pe[pp]:
                bd.w(sy, "pe", war_gate_pe[pp])
            bd.dmat(sy, "dtrA" if pp == 0 else "dtrB", rfm[pp][:], retn[rsl, :])
            tr_tick[pp] = bd.cnt["dtrA" if pp == 0 else "dtrB"]
            if war_gb4[pp]:
                bd.w(sy, "pool", war_gb4[pp])
            for dc in range(DC):
                bd.dma(sy, "dld2A" if pp == 0 else "dld2B", gb4[pp][:, dc, :], gf[dc][:, rsl])
            gl_tick[pp] = bd.cnt["dld2A" if pp == 0 else "dld2B"]

        pf4b(0)
        for qg in range(4):
            pp = qg % 2
            if qg + 1 < 4:
                pf4b(qg + 1)
            if war_xres:
                bd.w(sy, "dve", war_xres)
            for qs in range(4):
                bd.dma(sy, "dldA", xres[:, qs, :],
                       x_in[qg * 512 + qs * 128:qg * 512 + (qs + 1) * 128, :])
            if war_y:
                bd.w(v, "dsp", war_y)
            bd.w(v, "dldA")
            for qs in range(4):
                war_xres = bd.inc(v.tensor_tensor(xres[:, qs, :], xres[:, qs, :], bo_bc[:],
                                                  ALU.add), "dve")
            for wc in range(4):
                wpp = wc % 4
                gs = ("dgB", "dgC", "dgD", "dld2C")[wpp]
                if war_wb4[wpp]:
                    bd.w(p, "pe", war_wb4[wpp])
                bd.dma(p, gs, wb4[wpp][:],
                       watt["wo"].rearrange("(ic p) o -> p ic o", p=128)[:, :, wc * 256:(wc + 1) * 256])
            bd.w(p, "dtrA" if pp == 0 else "dtrB", tr_tick[pp])
            bd.w(p, "dld2A" if pp == 0 else "dld2B", gl_tick[pp])
            glast = 0
            for dc in range(DC):
                dv1 = bd.inc(p.tensor_scalar(rfm[pp][:, dc, :], rfm[pp][:, dc, :],
                                             col["gnw"][:, dc:dc + 1], col["gnb"][:, dc:dc + 1],
                                             ALU.mult, ALU.add), "pool")
                bd.w(p, "pool", dv1)
                if war_gate_pe[pp]:
                    bd.w(p, "pe", war_gate_pe[pp])
                glast = bd.inc(p.tensor_tensor(rfm[pp][:, dc, :], rfm[pp][:, dc, :],
                                               gb4[pp][:, dc, :], ALU.mult), "pool")
            war_rfm[pp] = glast
            war_gb4[pp] = glast
            for wc in range(8):
                wpp = wc % 4
                gs = ("dgB", "dgC", "dgD", "dld2C")[wpp]
                if wc >= 4:
                    if war_wb4[wpp]:
                        bd.w(p, "pe", war_wb4[wpp])
                    bd.dma(p, gs, wb4[wpp][:],
                           watt["wo"].rearrange("(ic p) o -> p ic o", p=128)[:, :, wc * 256:(wc + 1) * 256])
                bd.w(t, gs)
                bd.w(t, "pool", glast)
                for qs in range(4):
                    if war_psAz[qs]:
                        bd.w(t, "dve", war_psAz[qs])
                    for ic in range(DC - 1):
                        t.matmul(psA[qs][:, 0:256], rfm[pp][:, ic, qs * 128:(qs + 1) * 128],
                                 wb4[wpp][:, ic, :], start=(ic == 0), stop=False)
                    pe_mm = bd.inc(t.matmul(psA[qs][:, 0:256],
                                            rfm[pp][:, DC - 1, qs * 128:(qs + 1) * 128],
                                            wb4[wpp][:, DC - 1, :], start=False, stop=True), "pe")
                    war_wb4[wpp] = pe_mm
                    war_gate_pe[pp] = pe_mm
                    bd.w(v, "pe", pe_mm)
                    bd.w(v, "dldA")
                    dz = bd.inc(v.tensor_tensor(zb[:, qs, wc * 256:(wc + 1) * 256],
                                                psA[qs][:, 0:256],
                                                xres[:, qs, wc * 256:(wc + 1) * 256], ALU.add), "dve")
                    war_psAz[qs] = dz
                    war_xres = dz
            for qs in range(4):
                bd.w(v, "dve")
                dv = 0
                for c4 in range(4):
                    dv = bd.inc(v.bn_stats(st4[:, c4, :], zb[:, qs, c4 * 512:(c4 + 1) * 512]), "dve")
                bd.w(v, "dve", dv)
                dv = bd.inc(v.bn_aggr(mv4[:], st4.ap().rearrange("p a b -> p (a b)")), "dve")
                bd.w(a, "dve", dv)
                at = bd.inc(a.activation(sc4[:, 1:2], mv4[:, 1:2], AF.Sqrt, bias=c_eps[:, 0:1]), "act")
                bd.w(v, "act", at)
                dv = bd.inc(v.reciprocal(sc4[:, 2:3], sc4[:, 1:2]), "dve")
                bd.w(v, "dve", dv)
                dy = bd.inc(v.tensor_scalar(zb[:, qs, :], zb[:, qs, :], mv4[:, 0:1], sc4[:, 2:3],
                                            ALU.subtract, ALU.mult), "dve")
                bd.w(p, "dve", dy)
                rr = slice(qg * 512 + qs * 128, qg * 512 + (qs + 1) * 128)
                war_y = bd.dma(p, "dsp", y32[rr, :], zb[:, qs, :])
            bd.w(p, "dsp", war_y)
            bd.dma(p, "dspC", ybf[qg * 512:(qg + 1) * 512, :], y32[qg * 512:(qg + 1) * 512, :])
        barrier()
        p4b.close()

    pers.close()
    if upto >= 5:
        p5 = ExitStack()

        def sb5(name, shape, dt, **kw):
            return p5.enter_context(nc.sbuf_tensor(name, shape, dt, **kw))

        bd.w(sy, "dspC")

        yfm = sb5("p5_yfm", [128, DC, 512], BF16, is_transpose_dest=True)
        wgt = [sb5("p5_wg0", [128, DC, 256], BF16, is_transpose_dest=True),
               sb5("p5_wg1", [128, DC, 256], BF16, is_transpose_dest=True)]
        wut = [sb5("p5_wu0", [128, DC, 256], BF16, is_transpose_dest=True),
               sb5("p5_wu1", [128, DC, 256], BF16, is_transpose_dest=True)]
        h1 = sb5("p5_h1", [128, HC, 512], BF16)
        sg5 = [sb5("p5_sg0", [128, 512], F32), sb5("p5_sg1", [128, 512], F32)]
        wdt = [sb5("p5_wd0", [128, HC // 2, 512], BF16, is_transpose_dest=True),
               sb5("p5_wd1", [128, HC // 2, 512], BF16, is_transpose_dest=True)]
        wd_ap = wbf["d"]
        yres = sb5("p5_y", [128, 4, D], F32)
        z5 = sb5("p5_z", [128, 4, D], F32)
        st5 = sb5("p5_st", [128, 4, 6], F32)
        mv5 = sb5("p5_mv", [128, 2], F32)
        sc5 = sb5("p5_sc", [128, 4], F32)
        war_yfm = 0
        war_wgt = [0, 0]
        war_wdt = [0, 0]
        war_h1_pe = 0
        war_sgA = [0, 0]
        war_psRu = [0, 0]
        war_sg5 = [0, 0]
        war_psAd = [0] * 4
        war_yres = 0
        war_o5 = 0
        for grp in range(4):
            rsl = slice(grp * 512, (grp + 1) * 512)
            if war_yfm:
                bd.w(sy, "pe", war_yfm)
            bd.dmat(sy, "dtr", yfm[:], ybf[rsl, :])
            for hb in range(22):
                wpp = hb % 2
                if war_wgt[wpp]:
                    bd.w(sy, "pe", war_wgt[wpp])
                bd.dmat(sy, "dtr", wgt[wpp][:], wbf["g"][hb * 256:(hb + 1) * 256, :])
                bd.dmat(sy, "dtr", wut[wpp][:], wbf["u"][hb * 256:(hb + 1) * 256, :])
                bd.w(t, "dtr")
                for hc2 in range(2):
                    hc = hb * 2 + hc2
                    pp = hc % 2
                    hsl = slice(hc2 * 128, (hc2 + 1) * 128)
                    if war_sgA[pp]:
                        bd.w(t, "act", war_sgA[pp])
                    if war_psAd[pp]:
                        bd.w(t, "dve", war_psAd[pp])
                    for ic in range(DC - 1):
                        t.matmul(psA[pp][:], wgt[wpp][:, ic, hsl], yfm[:, ic, :],
                                 start=(ic == 0), stop=False)
                    peg = bd.inc(t.matmul(psA[pp][:], wgt[wpp][:, DC - 1, hsl],
                                          yfm[:, DC - 1, :], start=False, stop=True), "pe")
                    if war_psRu[pp]:
                        bd.w(t, "dve", war_psRu[pp])
                    for ic in range(DC - 1):
                        t.matmul(psR[pp][:], wut[wpp][:, ic, hsl], yfm[:, ic, :],
                                 start=(ic == 0), stop=False)
                    peu = bd.inc(t.matmul(psR[pp][:], wut[wpp][:, DC - 1, hsl],
                                          yfm[:, DC - 1, :], start=False, stop=True), "pe")
                    war_wgt[wpp] = peu
                    war_yfm = peu
                    bd.w(a, "pe", peg)
                    if war_sg5[pp]:
                        bd.w(a, "dve", war_sg5[pp])
                    at = bd.inc(a.activation(sg5[pp][:], psA[pp][:], AF.Silu), "act")
                    war_sgA[pp] = at
                    bd.w(v, "act", at)
                    bd.w(v, "pe", peu)
                    if war_h1_pe:
                        bd.w(v, "pe", war_h1_pe)
                    dh = bd.inc(v.tensor_tensor(h1[:, hc, :], sg5[pp][:], psR[pp][:],
                                                ALU.mult), "dve")
                    war_sg5[pp] = dh
                    war_psRu[pp] = dh
            bd.w(t, "act")
            if war_yres:
                bd.w(a, "dve", war_yres)
            if war_o5:
                bd.w(a, "dst", war_o5)
            for ts in range(4):
                bd.dma(a, "dld2A", yres[:, ts, :],
                       y32[grp * 512 + ts * 128:grp * 512 + (ts + 1) * 128, :])
            HHC = HC // 2
            for db in range(4):
                dsl = slice(db * 512, (db + 1) * 512)
                for hcb in range(2):
                    ds_ = "dtrA" if hcb == 0 else "dtrB"
                    if war_wdt[hcb]:
                        bd.w(sy, "pe", war_wdt[hcb])
                    bd.dmat(sy, ds_, wdt[hcb][:],
                            wd_ap[dsl, hcb * 2816:(hcb + 1) * 2816])
                bd.w(t, "dve")
                for hcb in range(2):
                    ds_ = "dtrA" if hcb == 0 else "dtrB"
                    bd.w(t, ds_)
                    pe_half = 0
                    for hc2 in range(HHC):
                        hc = hcb * HHC + hc2
                        for ts in range(4):
                            mm = t.matmul(psA[ts][:], h1[:, hc, ts * 128:(ts + 1) * 128],
                                          wdt[hcb][:, hc2, :],
                                          start=(hc == 0), stop=(hc == HC - 1))
                            if hc2 == HHC - 1 and ts == 3:
                                pe_half = bd.inc(mm, "pe")
                    war_wdt[hcb] = pe_half
                for ts in range(4):
                    bd.w(v, "pe")
                    bd.w(v, "dld2A")
                    dz = bd.inc(v.tensor_tensor(z5[:, ts, dsl], psA[ts][:],
                                                yres[:, ts, dsl], ALU.add), "dve")
                    war_psAd[ts] = dz
                    war_yres = dz
            for ts in range(4):
                bd.w(v, "dve")
                dv = 0
                for c4 in range(4):
                    dv = bd.inc(v.bn_stats(st5[:, c4, :], z5[:, ts, c4 * 512:(c4 + 1) * 512]), "dve")
                bd.w(v, "dve", dv)
                dv = bd.inc(v.bn_aggr(mv5[:], st5.ap().rearrange("p a b -> p (a b)")), "dve")
                bd.w(a, "dve", dv)
                at = bd.inc(a.activation(sc5[:, 1:2], mv5[:, 1:2], AF.Sqrt, bias=c_eps[:, 0:1]), "act")
                bd.w(v, "act", at)
                dv = bd.inc(v.reciprocal(sc5[:, 2:3], sc5[:, 1:2]), "dve")
                bd.w(v, "dve", dv)
                if war_o5:
                    bd.w(v, "dst", war_o5)
                dy = bd.inc(v.tensor_scalar(yres[:, ts, :], z5[:, ts, :], mv5[:, 0:1], sc5[:, 2:3],
                                            ALU.subtract, ALU.mult), "dve")
                bd.w(p, "dve", dy)
                rr = slice(grp * 512 + ts * 128, grp * 512 + (ts + 1) * 128)
                p.dma_start(out=out[rr, :], in_=yres[:, ts, :]).then_inc(bd.sem["dst"], 16)
                bd.cnt["dst"] += 16
                war_o5 = bd.cnt["dst"]
        p5.close()

    bd.w(p, "dst")
    bd.w(p, "dsp")
    bd.w(sy, "dst")
    bd.close()
    return nc


def _in_maps(inputs):
    tb = _host_tables()
    f32c = lambda k: np.ascontiguousarray(inputs[k], dtype=np.float32)
    common = dict(
        wq=f32c("L0_attn_w_q"), wk=f32c("L0_attn_w_k"), wv=f32c("L0_attn_w_v"),
        wg=f32c("L0_attn_w_g"), wo=f32c("L0_attn_w_o"),
        bq_r=np.asarray(inputs["L0_attn_b_q"], np.float32).reshape(1, D),
        bk_r=np.asarray(inputs["L0_attn_b_k"], np.float32).reshape(1, D),
        bg_r=np.asarray(inputs["L0_attn_b_g"], np.float32).reshape(1, D),
        bv_r=np.asarray(inputs["L0_attn_b_v"], np.float32).reshape(1, D),
        bo_r=np.asarray(inputs["L0_attn_b_o"], np.float32).reshape(1, D),
        gnw_r=np.asarray(inputs["L0_attn_gn_weight"], np.float32).reshape(1, D),
        gnb_r=np.asarray(inputs["L0_attn_gn_bias"], np.float32).reshape(1, D),
        sd_r=np.asarray(inputs["L0_ffn_s_down"], np.float32).reshape(1, D),
        sg_r=np.asarray(inputs["L0_ffn_s_gate"], np.float32).reshape(1, HID),
        su_r=np.asarray(inputs["L0_ffn_s_up"], np.float32).reshape(1, HID),
        wqg=np.ascontiguousarray(inputs["L0_ffn_wq_gate"], dtype=np.int32),
        wqu=np.ascontiguousarray(inputs["L0_ffn_wq_up"], dtype=np.int32),
        wqd=np.ascontiguousarray(inputs["L0_ffn_wq_down"], dtype=np.int32),
        cosq=tb["cosq"], sinq=tb["sinq"], cosk=tb["cosk"], sink=tb["sink"],
        trimask=tb["trimask"], perm=tb["perm"], ones1=tb["ones1"], onesf=tb["onesf"],
    )
    x_full = np.asarray(inputs["L0_attn_x"], np.float32)
    return [dict(common, x=np.ascontiguousarray(x_full[c])) for c in range(B)]


def kernel(**inputs):
    in_maps = _in_maps(inputs)
    nc = build()
    res = run_bass_kernel_spmd(nc, in_maps, core_ids=list(range(B)))
    return np.stack([res.results[c]["out"] for c in range(B)], axis=0)
